# version 04
# baseline (speedup 1.0000x reference)
import numpy as np
import ml_dtypes
import concourse.bass as bass
import concourse.mybir as mybir
from concourse.bass import ds
from concourse.bass_utils import run_bass_kernel_spmd
import concourse.bass_utils as _bu

if not getattr(_bu, "_ldw_patched", False):
    _orig_run_command = _bu.run_command

    def _run_command_ldw(argv, **kwargs):
        argv = [a.replace("--enable-ldw-opt=false", "--enable-ldw-opt=true")
                if isinstance(a, str) else a for a in argv]
        return _orig_run_command(argv, **kwargs)

    _bu.run_command = _run_command_ldw
    _bu._ldw_patched = True

F32 = mybir.dt.float32
BF16 = mybir.dt.bfloat16
AF = mybir.ActivationFunctionType
ALU = mybir.AluOpType
AX = mybir.AxisListType

B, T, D = 2, 4096, 2048
NH, DH = 16, 128
TC = 1024
HPC = 4
DCN = 16
SCALE = float(1.0 / np.sqrt(128.0))
NEG = -1.0e30
EPS = 1e-5
GROUPS = [[0, 1, 2, 3], [4, 5, 6, 7]]

LCFG = [
    dict(idx=0, HF=8192, scales=False),
    dict(idx=1, HF=5632, scales=True),
]

PROJ = dict(XNF0=0, XNF1=8192, WQ=16384, WK=24576, WV=32768,
            QSB0=40960, QSB1=41472, ROUT0=41984, ROUT1=42496,
            VSB0=43008, VSB1=43520)
ATTN = dict(K0=0, K1=4096, V0=8192, V1=12288, Q0=16384, P0=20480, P1=24576,
            P2=28672, O0=32768, O1=36864, PTS0=40960, PTS1=41984)
NFPF = 6
FFN = dict(HFM=0, DW0=16384, DW1=43008,
           **{f"GW{i}": 18432 + i * 2048 for i in range(NFPF)},
           **{f"UW{i}": 30720 + i * 2048 for i in range(NFPF)})
WO = dict(OLA=0, OLB=8192, WOC0=16384, WOC1=17408)
ATTN["DG0"] = 43008
ATTN["DG1"] = 43520
ATTN["DG2"] = 44032
AFO = dict(T1=0, T2=512, COS0=1024, COS1=1536, SIN0=2048, SIN1=2560,
           GSF0=0, GSF1=512, USF0=1024, USF1=1536, DTMP=2048)
SS_M, SS_CM, SS_BB, SS_S, SS_DD, SS_F, SS_TOT, SS_RR = 0, 4, 5, 8, 12, 16, 20, 21


def cdiv(a, b):
    return (a + b - 1) // b


class TK:

    ENGINES = ("gpsimd", "scalar", "tensor", "vector", "sync")

    def __init__(self):
        self.cnt = {}
        self.wm = {e: {} for e in self.ENGINES}
        self.bufs = {}
        self.prog = {e: [] for e in self.ENGINES}
        self.pending_fence = {e: None for e in self.ENGINES}

    def fence(self):
        snap = {s: c for s, c in self.cnt.items() if c}
        for e in self.ENGINES:
            self.pending_fence[e] = dict(snap)

    def emit(self, eng, fns, reads=(), writes=(), stream=None, inc=1):
        if not isinstance(fns, (list, tuple)):
            fns = [fns]
        need = {}
        pf = self.pending_fence[eng]
        if pf:
            for s, c in pf.items():
                need[s] = max(need.get(s, 0), c)
            self.pending_fence[eng] = None
        for k in reads:
            b = self.bufs.get(k)
            if b and b["w"] is not None:
                s, c = b["w"]
                need[s] = max(need.get(s, 0), c)
        for k in writes:
            b = self.bufs.get(k)
            if b:
                if b["w"] is not None:
                    s, c = b["w"]
                    need[s] = max(need.get(s, 0), c)
                for s, c in b["r"].items():
                    need[s] = max(need.get(s, 0), c)
        wm = self.wm[eng]
        waits = {}
        for s, c in need.items():
            if c > wm.get(s, 0):
                waits[s] = c
                wm[s] = c
        mc = None
        if fns and stream is not None:
            self.cnt[stream] = self.cnt.get(stream, 0) + inc
            mc = (stream, self.cnt[stream])
        self.prog[eng].append((waits, fns, stream if mc else None, inc))
        for k in reads:
            b = self.bufs.setdefault(k, {"w": None, "r": {}})
            if mc:
                b["r"][mc[0]] = mc[1]
        for k in writes:
            assert mc is not None, f"writer of {k} must signal"
            self.bufs[k] = {"w": mc, "r": {}}


def build_nc(lids=(0, 1), phases=("ln", "ffn", "proj", "attn", "wo", "cc")):
    nc = bass.Bass("TRN2", target_bir_lowering=False)
    tk = TK()

    xin = nc.dram_tensor("xin", [D, TC], F32, kind="ExternalInput").ap()
    xout = nc.dram_tensor("xout", [D, TC], F32, kind="ExternalOutput").ap()
    cos2 = nc.dram_tensor("cos2", [128, T], F32, kind="ExternalInput").ap()
    sin2 = nc.dram_tensor("sin2", [128, T], F32, kind="ExternalInput").ap()
    ident_in = nc.dram_tensor("ident", [128, 128], BF16, kind="ExternalInput").ap()
    rmat_in = nc.dram_tensor("rmat", [128, 128], BF16, kind="ExternalInput").ap()
    tri_in = nc.dram_tensor("tri", [128, 128], F32, kind="ExternalInput").ap()
    W = {}
    for l in lids:
        cfg = LCFG[l]
        W[l, "wq"] = nc.dram_tensor(f"wq{l}", [D, 512], F32, kind="ExternalInput").ap()
        W[l, "wk"] = nc.dram_tensor(f"wk{l}", [D, 512], F32, kind="ExternalInput").ap()
        W[l, "wv"] = nc.dram_tensor(f"wv{l}", [D, 512], F32, kind="ExternalInput").ap()
        W[l, "woA"] = nc.dram_tensor(f"woA{l}", [1024, D], F32, kind="ExternalInput").ap()
        W[l, "woB"] = nc.dram_tensor(f"woB{l}", [1024, D], F32, kind="ExternalInput").ap()
        W[l, "g"] = nc.dram_tensor(f"g{l}", [D, cfg["HF"]], F32, kind="ExternalInput").ap()
        W[l, "u"] = nc.dram_tensor(f"u{l}", [D, cfg["HF"]], F32, kind="ExternalInput").ap()
        W[l, "d"] = nc.dram_tensor(f"d{l}", [cfg["HF"], D], F32, kind="ExternalInput").ap()
        if cfg["scales"]:
            nht = cfg["HF"] // 128
            W[l, "gs"] = nc.dram_tensor(f"gs{l}", [128, nht], F32, kind="ExternalInput").ap()
            W[l, "us"] = nc.dram_tensor(f"us{l}", [128, nht], F32, kind="ExternalInput").ap()
            W[l, "ds"] = nc.dram_tensor(f"ds{l}", [128, 16], F32, kind="ExternalInput").ap()

    d_xnin = nc.dram_tensor("d_xnin", [D, TC], BF16, kind="Internal")
    d_xnfull = nc.dram_tensor("d_xnfull", [4 * D, TC], BF16, kind="Internal")
    d_qk = nc.dram_tensor("d_qk", [2, HPC, 128, T], BF16, kind="Internal")
    d_v = nc.dram_tensor("d_v", [HPC, 128, 32, 128], BF16, kind="Internal")
    d_oin = [nc.dram_tensor(f"d_o{h}in", [256, T], BF16, kind="Internal") for h in range(2)]
    d_oout = [nc.dram_tensor(f"d_o{h}out", [1024, T], BF16, kind="Internal") for h in range(2)]

    import contextlib
    stack = contextlib.ExitStack()
    sb = lambda name, shape, dt: stack.enter_context(nc.sbuf_tensor(name, shape, dt))
    ps = lambda name, shape, dt: stack.enter_context(nc.psum_tensor(name, shape, dt))

    x_sb = sb("x_sb", [128, DCN * TC], F32)
    xn_sb = sb("xn_sb", [128, DCN * TC], BF16)
    AB = sb("AB", [128, 45056], BF16)
    AFT = sb("AFT", [128, 3072], F32)
    ln_xbf = sb("ln_xbf", [128, 1024], BF16)
    ln_x2 = sb("ln_x2", [128, 1024], BF16)
    SS = sb("SS", [128, 32], F32)
    ident_sb = sb("ident_sb", [128, 128], BF16)
    rmat_sb = sb("rmat_sb", [128, 128], BF16)
    tri_sb = sb("tri_sb", [128, 128], F32)
    onesb = sb("onesb", [128, 1], BF16)
    onesf = sb("onesf", [1, 128], F32)
    sc_sb = {}
    for l in lids:
        if LCFG[l]["scales"]:
            nht = LCFG[l]["HF"] // 128
            sc_sb[l, "gs"] = sb(f"gs_sb{l}", [128, nht], F32)
            sc_sb[l, "us"] = sb(f"us_sb{l}", [128, nht], F32)
            sc_sb[l, "ds"] = sb(f"ds_sb{l}", [128, 16], F32)

    sA = ps("sA", [128, 1024], F32)
    sB = ps("sB", [128, 1024], F32)
    ptA = ps("ptA", [128, 512], F32)
    ptB = ps("ptB", [128, 512], F32)
    oPA = ps("oPA", [128, 512], F32)
    oPB = ps("oPB", [128, 512], F32)

    sems = {}

    def Ab(off, n):
        return AB[:, off:off + n]

    def Af(off, n):
        return AFT[:, off:off + n]

    E = tk.emit

    def dram3(ap2, p=128):
        return ap2.rearrange("(n p) t -> p n t", p=p)

    def sb3(ap2, inner):
        return ap2.rearrange("p (n t) -> p n t", t=inner)

    E("vector", lambda e: e.memset(onesb[:], 1.0), writes=["onesb"], stream="dve")
    E("vector", lambda e: e.memset(onesf[:], 1.0), writes=["onesf"], stream="dve")
    E("sync", lambda e: e.dma_start(out=ident_sb[:], in_=ident_in),
      writes=["ident"], stream="q_ident", inc=16)
    E("sync", lambda e: e.dma_start(out=rmat_sb[:], in_=rmat_in),
      writes=["rmat"], stream="q_rmat", inc=16)
    E("sync", lambda e: e.dma_start(out=tri_sb[:], in_=tri_in),
      writes=["tri"], stream="q_tri", inc=16)
    E("sync", lambda e, a=dram3(xin), o=sb3(x_sb[:, 0:DCN * TC], TC): e.dma_start(out=o, in_=a),
      writes=[f"x{i}" for i in range(DCN)], stream="q_xin", inc=16)
    for l in lids:
        if LCFG[l]["scales"]:
            for nm in ("gs", "us", "ds"):
                E("sync", lambda e, o=sc_sb[l, nm], i=W[l, nm]: e.dma_start(out=o[:], in_=i),
                  writes=[f"{nm}{l}"], stream=f"q_{nm}{l}", inc=16)

    def emit_ffn_prefetch(l):
        for ht in range(min(NFPF, LCFG[l]["HF"] // 128)):
            gw = Ab(FFN[f"GW{ht}"], 2048)
            uw = Ab(FFN[f"UW{ht}"], 2048)
            gsrc = dram3(W[l, "g"][:, ht * 128:(ht + 1) * 128])
            usrc = dram3(W[l, "u"][:, ht * 128:(ht + 1) * 128])
            E("gpsimd", lambda e, o=sb3(gw, 128), i=gsrc: e.dma_start(out=o, in_=i),
              writes=[f"gw{ht}"], stream=f"q_gw{ht}", inc=16)
            E("gpsimd", lambda e, o=sb3(uw, 128), i=usrc: e.dma_start(out=o, in_=i),
              writes=[f"uw{ht}"], stream=f"q_uw{ht}", inc=16)

    emit_ffn_prefetch(lids[0])

    for l in lids:
        cfg = LCFG[l]
        HF = cfg["HF"]
        HT = HF // 128
        HTQ = HT // 4

        if "ln" not in phases:
            continue
        for tc2 in range(2):
            for dc in range(DCN):
                par = dc % 2
                xbf = ln_xbf[:, par * 512:(par + 1) * 512]
                x2bf = ln_x2[:, par * 512:(par + 1) * 512]
                xt = x_sb[:, dc * TC + tc2 * 512: dc * TC + tc2 * 512 + 512]
                E("vector", lambda e, o=xbf, i=xt: e.tensor_copy(o, i),
                  reads=[f"x{dc}"], writes=[f"lnxbf{par}"], stream="dve")
                E("scalar", lambda e, o=x2bf, i=xt: e.activation(o, i, AF.Square),
                  reads=[f"x{dc}"], writes=[f"lnx2bf{par}"], stream="act")
                E("tensor", lambda e, o=oPA[0:1, 0:512], i=xbf, st=(dc == 0), sp=(dc == DCN - 1):
                  e.matmul(o, onesb[:], i, start=st, stop=sp),
                  reads=[f"lnxbf{par}", "onesb"], writes=["oPA"], stream="pe")
                E("tensor", lambda e, o=oPB[0:1, 0:512], i=x2bf, st=(dc == 0), sp=(dc == DCN - 1):
                  e.matmul(o, onesb[:], i, start=st, stop=sp),
                  reads=[f"lnx2bf{par}", "onesb"], writes=["oPB"], stream="pe")
            msf = AFT[0:1, 0:512]
            v1t = AFT[0:1, 512:1024]
            m2t = AFT[0:1, 1024:1536]
            lnv = AFT[0:1, 1024:1536]
            rst = AFT[0:1, 1536:2048]
            E("vector", lambda e, o=msf: e.tensor_scalar_mul(o, oPA[0:1, 0:512], 1.0 / 2048.0),
              reads=["oPA"], writes=["ln0"], stream="dve")
            E("vector", lambda e, o=v1t: e.tensor_scalar(o, oPB[0:1, 0:512], 1.0 / 2048.0, EPS,
                                                         ALU.mult, ALU.add),
              reads=["oPB"], writes=["lnA"], stream="dve")
            E("vector", lambda e, o=m2t, i=msf: e.tensor_mul(o, i, i),
              reads=["ln0"], writes=["lnB"], stream="dve")
            E("vector", lambda e, o=v1t, i=m2t: e.tensor_sub(o, o, i),
              reads=["lnA", "lnB"], writes=["lnA"], stream="dve")
            E("tensor", lambda e, i=msf: e.matmul(sA[:, 0:512], onesf[:], i, start=True, stop=True),
              reads=["ln0", "onesf"], writes=["sA0"], stream="pe")
            E("scalar", lambda e, o=lnv, i=v1t: e.activation(o, i, AF.Ln),
              reads=["lnA"], writes=["lnB"], stream="act")
            E("scalar", lambda e, o=rst, i=lnv: e.activation(o, i, AF.Exp, scale=-0.5),
              reads=["lnB"], writes=["lnC"], stream="act")
            E("tensor", lambda e, i=rst: e.matmul(sB[:, 0:512], onesf[:], i, start=True, stop=True),
              reads=["lnC", "onesf"], writes=["sB0"], stream="pe")
            for dc in range(DCN):
                xt = x_sb[:, dc * TC + tc2 * 512: dc * TC + tc2 * 512 + 512]
                xnt = xn_sb[:, dc * TC + tc2 * 512: dc * TC + tc2 * 512 + 512]
                E("vector", lambda e, o=Af(2048, 512), i=xt: e.tensor_sub(o, i, sA[:, 0:512]),
                  reads=[f"x{dc}", "sA0"], writes=["ln_xt"], stream="dve")
                E("vector", lambda e, o=xnt, i=Af(2048, 512): e.tensor_mul(o, i, sB[:, 0:512]),
                  reads=["ln_xt", "sB0"], writes=["xn"], stream="dve")
        E("sync", lambda e, o=dram3(d_xnin.ap()), i=sb3(xn_sb[:, 0:DCN * TC], TC): e.dma_start(out=o, in_=i),
          reads=["xn"], writes=["d_xnin"], stream="q_xnin", inc=16)
        if "cc" in phases:
            E("gpsimd", lambda e: e.collective_compute(
                "AllGather", ALU.bypass, replica_groups=GROUPS,
                ins=[d_xnin.ap().opt()], outs=[d_xnfull.ap().opt()]),
              reads=["d_xnin"], writes=["d_xnfull"], stream="cc_xn")

        tk.fence()
        if "ffn" not in phases:
            continue
        for qtr in range(4):
            for j in range(HTQ):
                ht = qtr * HTQ + j
                par = ht % NFPF
                gw = Ab(FFN[f"GW{par}"], 2048)
                uw = Ab(FFN[f"UW{par}"], 2048)
                if ht >= NFPF:
                    gsrc = dram3(W[l, "g"][:, ht * 128:(ht + 1) * 128])
                    usrc = dram3(W[l, "u"][:, ht * 128:(ht + 1) * 128])
                    E("gpsimd", lambda e, o=sb3(gw, 128), i=gsrc: e.dma_start(out=o, in_=i),
                      writes=[f"gw{par}"], stream=f"q_gw{par}", inc=16)
                    E("gpsimd", lambda e, o=sb3(uw, 128), i=usrc: e.dma_start(out=o, in_=i),
                      writes=[f"uw{par}"], stream=f"q_uw{par}", inc=16)
                fns = [(lambda e, o=sA[:, tc2 * 512:(tc2 + 1) * 512],
                        w=gw[:, dc * 128:(dc + 1) * 128],
                        r=xn_sb[:, dc * TC + tc2 * 512: dc * TC + tc2 * 512 + 512],
                        st=(dc == 0), sp=(dc == DCN - 1):
                        e.matmul(o, w, r, start=st, stop=sp, skip_group_check=True))
                       for dc in range(DCN) for tc2 in range(2)]
                E("tensor", fns, reads=[f"gw{par}", "xn"], writes=["sA0", "sA1"], stream="pe")
                fns = [(lambda e, o=sB[:, tc2 * 512:(tc2 + 1) * 512],
                        w=uw[:, dc * 128:(dc + 1) * 128],
                        r=xn_sb[:, dc * TC + tc2 * 512: dc * TC + tc2 * 512 + 512],
                        st=(dc == 0), sp=(dc == DCN - 1):
                        e.matmul(o, w, r, start=st, stop=sp, skip_group_check=True))
                       for dc in range(DCN) for tc2 in range(2)]
                E("tensor", fns, reads=[f"uw{par}", "xn"], writes=["sB0", "sB1"], stream="pe")
                for tc2 in range(2):
                    gps = sA[:, tc2 * 512:(tc2 + 1) * 512]
                    ups = sB[:, tc2 * 512:(tc2 + 1) * 512]
                    gk, uk = f"sA{tc2}", f"sB{tc2}"
                    gsf = Af(AFO[f"GSF{tc2}"], 512)
                    usf = Af(AFO[f"USF{tc2}"], 512)
                    if cfg["scales"]:
                        gsc = sc_sb[l, "gs"][:, ht:ht + 1]
                        usc = sc_sb[l, "us"][:, ht:ht + 1]
                        E("scalar", lambda e, o=gsf, i=gps, s=gsc: e.activation(o, i, AF.Silu, scale=s),
                          reads=[gk, f"gs{l}"], writes=[f"gsf{tc2}"], stream="act")
                        E("scalar", lambda e, o=usf, i=ups, s=usc: e.activation(o, i, AF.Copy, scale=s),
                          reads=[uk, f"us{l}"], writes=[f"usf{tc2}"], stream="act")
                    else:
                        E("scalar", lambda e, o=gsf, i=gps: e.activation(o, i, AF.Silu),
                          reads=[gk], writes=[f"gsf{tc2}"], stream="act")
                        E("scalar", lambda e, o=usf, i=ups: e.activation(o, i, AF.Copy),
                          reads=[uk], writes=[f"usf{tc2}"], stream="act")
                    hdst = Ab(FFN["HFM"] + j * 1024 + tc2 * 512, 512)
                    E("vector", lambda e, o=hdst, a=gsf, b=usf: e.tensor_mul(o, a, b),
                      reads=[f"gsf{tc2}", f"usf{tc2}"], writes=["hfm"], stream="dve")
            for oc in range(16):
                par = oc % 2
                dw = Ab(FFN[f"DW{par}"], HTQ * 128)
                dsrc = dram3(W[l, "d"][qtr * HTQ * 128:(qtr + 1) * HTQ * 128,
                                       oc * 128:(oc + 1) * 128])
                E("gpsimd", lambda e, o=sb3(dw, 128), i=dsrc: e.dma_start(out=o, in_=i),
                  writes=[f"dw{par}"], stream=f"q_dw{par}", inc=16)
                fns = [(lambda e, o=(oPA if tc2 == 0 else oPB)[:, 0:512],
                        w=dw[:, hl * 128:(hl + 1) * 128],
                        r=Ab(FFN["HFM"] + hl * 1024 + tc2 * 512, 512),
                        st=(hl == 0), sp=(hl == HTQ - 1):
                        e.matmul(o, w, r, start=st, stop=sp, skip_group_check=True))
                       for hl in range(HTQ) for tc2 in range(2)]
                E("tensor", fns, reads=[f"dw{par}", "hfm"], writes=["oPA", "oPB"], stream="pe")
                for tc2 in range(2):
                    dps, dkey = (oPA, "oPA") if tc2 == 0 else (oPB, "oPB")
                    xsl = x_sb[:, oc * TC + tc2 * 512: oc * TC + tc2 * 512 + 512]
                    if cfg["scales"]:
                        dtmp = Af(AFO["DTMP"], 512)
                        dsc = sc_sb[l, "ds"][:, oc:oc + 1]
                        E("vector", lambda e, o=dtmp, i=dps[:, 0:512], s=dsc: e.tensor_scalar_mul(o, i, s),
                          reads=[dkey, f"ds{l}"], writes=["dtmp"], stream="dve")
                        E("vector", lambda e, o=xsl, i=dtmp: e.tensor_add(o, o, i),
                          reads=["dtmp", f"x{oc}"], writes=[f"x{oc}"], stream="dve")
                    else:
                        E("vector", lambda e, o=xsl, i=dps[:, 0:512]: e.tensor_add(o, o, i),
                          reads=[dkey, f"x{oc}"], writes=[f"x{oc}"], stream="dve")

        tk.fence()
        if "proj" not in phases:
            continue
        plvl = 4
        for p in phases:
            if p.startswith("plvl"):
                plvl = int(p[4:])
        for nm, off in (("wq", PROJ["WQ"]), ("wk", PROJ["WK"]), ("wv", PROJ["WV"])):
            wsrc = dram3(W[l, nm])
            E("gpsimd", lambda e, o=sb3(Ab(off, 8192), 512), i=wsrc: e.dma_start(out=o, in_=i),
              writes=[nm], stream=f"q_{nm}", inc=16)
        for tc in range(8):
            par = tc % 2
            rr, c0 = tc // 2, (tc % 2) * 512
            xnf = Ab(PROJ[f"XNF{par}"], 8192)
            xnfsrc = dram3(d_xnfull.ap()[2048 * rr:2048 * (rr + 1), c0:c0 + 512])
            E("sync", lambda e, o=sb3(xnf, 512), i=xnfsrc: e.dma_start(out=o, in_=i),
              reads=["d_xnfull"], writes=[f"xnf{par}"], stream=f"q_xnf{par}", inc=16)
            cosb = Af(AFO[f"COS{par}"], 512)
            sinb = Af(AFO[f"SIN{par}"], 512)
            E("sync", lambda e, o=cosb, i=cos2[:, tc * 512:(tc + 1) * 512]: e.dma_start(out=o, in_=i),
              writes=[f"cos{par}"], stream=f"q_cos{par}", inc=16)
            E("sync", lambda e, o=sinb, i=sin2[:, tc * 512:(tc + 1) * 512]: e.dma_start(out=o, in_=i),
              writes=[f"sin{par}"], stream=f"q_sin{par}", inc=16)
            for h in range(HPC):
                for s, (woff, wkey) in enumerate(((PROJ["WQ"], "wq"), (PROJ["WK"], "wk"))):
                    mps, mk = (sA[:, 0:512], "sA0") if s == 0 else (sB[:, 0:512], "sB0")
                    rps, rk = (sA[:, 512:1024], "sA1") if s == 0 else (sB[:, 512:1024], "sB1")
                    fns = [(lambda e, o=mps, w=Ab(woff + dc * 512 + h * 128, 128),
                            r=Ab(PROJ[f"XNF{par}"] + dc * 512, 512),
                            st=(dc == 0), sp=(dc == DCN - 1):
                            e.matmul(o, w, r, start=st, stop=sp)) for dc in range(DCN)]
                    E("tensor", fns, reads=[wkey, f"xnf{par}"], writes=[mk], stream="pe")
                    qsb = Ab(PROJ[f"QSB{s}"], 512)
                    E("scalar", lambda e, o=qsb, i=mps: e.activation(o, i, AF.Copy),
                      reads=[mk], writes=[f"qsb{s}", mk], stream="act")
                    if plvl < 2:
                        continue
                    E("tensor", lambda e, o=rps, i=qsb: e.matmul(o, rmat_sb[:], i, start=True, stop=True),
                      reads=[f"qsb{s}", "rmat"], writes=[rk], stream="pe")
                    if plvl < 3:
                        continue
                    t1 = Af(AFO["T1"], 512)
                    t2 = Af(AFO["T2"], 512)
                    rout = Ab(PROJ[f"ROUT{s}"], 512)
                    E("vector", lambda e, o=t1, i=mps, c=cosb: e.tensor_mul(o, i, c),
                      reads=[mk, f"cos{par}"], writes=["t1"], stream="dve")
                    E("vector", lambda e, o=t2, i=rps, c=sinb: e.tensor_mul(o, i, c),
                      reads=[rk, f"sin{par}"], writes=["t2"], stream="dve")
                    E("vector", lambda e, o=rout, a=t1, b=t2: e.tensor_add(o, a, b),
                      reads=["t1", "t2"], writes=[f"rout{s}"], stream="dve")
                    dst = d_qk.ap()[s, h, :, tc * 512:(tc + 1) * 512]
                    E("sync", lambda e, o=dst, i=rout: e.dma_start(out=o, in_=i),
                      reads=[f"rout{s}"], writes=["d_qk"], stream="q_dqk", inc=16)
            if plvl < 4:
                continue
            for tt in range(4):
                vps, vk = (oPA, "oPA") if tt % 2 == 0 else (oPB, "oPB")
                fns = [(lambda e, o=vps[:, 0:512],
                        w=Ab(PROJ[f"XNF{par}"] + dc * 512 + tt * 128, 128),
                        r=Ab(PROJ["WV"] + dc * 512, 512),
                        st=(dc == 0), sp=(dc == DCN - 1):
                        e.matmul(o, w, r, start=st, stop=sp)) for dc in range(DCN)]
                E("tensor", fns, reads=["wv", f"xnf{par}"], writes=[vk], stream="pe")
                vsb = Ab(PROJ[f"VSB{tt % 2}"], 512)
                E("vector", lambda e, o=vsb, i=vps[:, 0:512]: e.tensor_copy(o, i),
                  reads=[vk], writes=[f"vsb{tt % 2}"], stream="dve")
                TT = tc * 4 + tt
                dst = d_v.ap()[:, :, TT, :].transpose([1, 0, 2])
                E("sync", lambda e, o=dst, i=vsb.rearrange("p (h j) -> p h j", h=4):
                  e.dma_start(out=o, in_=i),
                  reads=[f"vsb{tt % 2}"], writes=["d_v"], stream="q_dv", inc=16)

        tk.fence()
        if "attn" not in phases:
            continue
        av_ctr = [0]

        def emit_loads(h):
            par = h % 2
            kb = Ab(ATTN[f"K{par}"], 4096)
            vb = Ab(ATTN[f"V{par}"], 4096)
            qb = Ab(ATTN["Q0"], 4096)
            E("sync", lambda e, o=kb, i=d_qk.ap()[1, h]: e.dma_start(out=o, in_=i),
              reads=["d_qk"], writes=[f"Kb{par}"], stream=f"q_K{par}", inc=16)
            E("sync", lambda e, o=qb, i=d_qk.ap()[0, h]: e.dma_start(out=o, in_=i),
              reads=["d_qk"], writes=["Qb"], stream="q_Q", inc=16)
            E("sync", lambda e, o=vb, i=d_v.ap()[h].rearrange("p tt j -> p (tt j)"):
              e.dma_start(out=o, in_=i),
              reads=["d_v"], writes=[f"Vb{par}"], stream=f"q_V{par}", inc=16)

        def emit_stats(h, qt):
            par = h % 2
            gi = (h * 32 + qt) % 3
            kb = Ab(ATTN[f"K{par}"], 4096)
            qb = Ab(ATTN["Q0"], 4096)
            Ecols = (qt + 1) * 128
            Nc = cdiv(Ecols, 1024)
            pP = Ab(ATTN[f"P{gi}"], 4096)
            pkey = f"P{gi}"
            for c in range(Nc):
                lo = c * 1024
                hi = min(Ecols, lo + 1024)
                w = hi - lo
                sps, skeys = (sA, ["sA0", "sA1"]) if c % 2 == 0 else (sB, ["sB0", "sB1"])
                fns = []
                for b0 in range(0, w, 512):
                    bw = min(512, w - b0)
                    fns.append(lambda e, o=sps[:, b0:b0 + bw],
                               q=qb[:, qt * 128:(qt + 1) * 128],
                               k=kb[:, lo + b0:lo + b0 + bw]:
                               e.matmul(o, q, k, start=True, stop=True))
                E("tensor", fns, reads=["Qb", f"Kb{par}"], writes=skeys, stream="pe")
                if hi == Ecols:
                    E("vector", lambda e, o=sps[:, w - 128:w]: e.tensor_add(o, o, tri_sb[:]),
                      reads=skeys + ["tri"], writes=skeys, stream="dve")
                cm = SS[:, SS_CM:SS_CM + 1]
                E("vector", lambda e, o=cm, i=sps[:, 0:w]: e.reduce_max(o, i, axis=AX.X),
                  reads=skeys, writes=["ss"], stream="dve")
                mc = SS[:, SS_M + c:SS_M + c + 1]
                if c == 0:
                    E("vector", lambda e, o=mc, i=cm: e.tensor_copy(o, i),
                      reads=["ss"], writes=["ss"], stream="dve")
                else:
                    mprev = SS[:, SS_M + c - 1:SS_M + c]
                    E("vector", lambda e, o=mc, a=mprev, b=cm: e.tensor_max(o, a, b),
                      reads=["ss"], writes=["ss"], stream="dve")
                bb = SS[:, SS_BB:SS_BB + 1]
                E("vector", lambda e, o=bb, i=mc: e.tensor_scalar_mul(o, i, -SCALE),
                  reads=["ss"], writes=["ss"], stream="dve")
                sc = SS[:, SS_S + c:SS_S + c + 1]
                E("scalar", lambda e, o=pP[:, lo:hi], i=sps[:, 0:w], b=bb, a=sc:
                  e.activation(o, i, AF.Exp, bias=b, scale=SCALE, accum_out=a),
                  reads=skeys + ["ss"], writes=[pkey, "ss"], stream="act")
            dgb = Ab(ATTN[f"DG{gi}"], 512)
            dgkey = f"dg{gi}"
            rr_ = SS[:, SS_RR:SS_RR + 1]
            if Nc == 1:
                E("vector", lambda e, o=rr_: e.reciprocal(o, SS[:, SS_S:SS_S + 1]),
                  reads=["ss"], writes=["ss"], stream="dve")
                E("vector", lambda e, o=dgb[:, 0:128], s=rr_:
                  e.tensor_scalar_mul(o, ident_sb[:], s),
                  reads=["ss", "ident"], writes=[dgkey], stream="dve")
            else:
                dd = SS[:, SS_DD:SS_DD + Nc]
                E("vector", lambda e, o=dd, i=SS[:, SS_M:SS_M + Nc],
                  s1=SS[:, SS_M + Nc - 1:SS_M + Nc]:
                  e.tensor_scalar(o, i, s1, SCALE, ALU.subtract, ALU.mult),
                  reads=["ss"], writes=["ss"], stream="dve")
                ff = SS[:, SS_F:SS_F + Nc]
                E("scalar", lambda e, o=ff, i=dd: e.activation(o, i, AF.Exp),
                  reads=["ss"], writes=["ss"], stream="act")
                E("vector", lambda e, o=dd, a=ff, b=SS[:, SS_S:SS_S + Nc]: e.tensor_mul(o, a, b),
                  reads=["ss"], writes=["ss"], stream="dve")
                tot = SS[:, SS_TOT:SS_TOT + 1]
                E("vector", lambda e, o=tot, i=dd: e.reduce_sum(o, i, axis=AX.X),
                  reads=["ss"], writes=["ss"], stream="dve")
                E("vector", lambda e, o=rr_, i=tot: e.reciprocal(o, i),
                  reads=["ss"], writes=["ss"], stream="dve")
                E("vector", lambda e, o=ff, s=rr_: e.tensor_scalar_mul(o, o, s),
                  reads=["ss"], writes=["ss"], stream="dve")
                for c in range(Nc):
                    fc = SS[:, SS_F + c:SS_F + c + 1]
                    E("vector", lambda e, o=dgb[:, c * 128:(c + 1) * 128], s=fc:
                      e.tensor_scalar_mul(o, ident_sb[:], s),
                      reads=["ss", "ident"], writes=[dgkey], stream="dve")

        def emit_av(h, qt):
            par = h % 2
            gi = (h * 32 + qt) % 3
            vb = Ab(ATTN[f"V{par}"], 4096)
            Ecols = (qt + 1) * 128
            nkt = Ecols // 128
            pP = Ab(ATTN[f"P{gi}"], 4096)
            pkey = f"P{gi}"
            dgb = Ab(ATTN[f"DG{gi}"], 512)
            dgkey = f"dg{gi}"
            ops_, okey = (oPA, "oPA") if qt % 2 == 0 else (oPB, "oPB")
            for base in range(0, nkt, 4):
                cnt = min(4, nkt - base)
                bpar = av_ctr[0] % 2
                av_ctr[0] += 1
                ptp, ptkey = (ptA, "ptA") if bpar == 0 else (ptB, "ptB")
                dgs = dgb[:, (base // 8) * 128:(base // 8) * 128 + 128]
                fns = [(lambda e, o=ptp[:, i * 128:(i + 1) * 128],
                        i_=pP[:, (base + i) * 128:(base + i + 1) * 128], d=dgs:
                        e.matmul(o, i_, d, start=True, stop=True)) for i in range(cnt)]
                E("tensor", fns, reads=[pkey, dgkey], writes=[ptkey], stream="pe")
                pts = Ab(ATTN[f"PTS{bpar}"], 1024)
                E("scalar", lambda e, o=pts[:, 0:cnt * 128], i=ptp[:, 0:cnt * 128]:
                  e.activation(o, i, AF.Copy),
                  reads=[ptkey], writes=[f"pts{bpar}", ptkey], stream="act")
                fns = [(lambda e, o=ops_[:, 0:128],
                        v=vb[:, (base + i) * 128:(base + i + 1) * 128],
                        r=pts[:, i * 128:(i + 1) * 128],
                        st=(base + i == 0), sp=(base + i == nkt - 1):
                        e.matmul(o, v, r, start=st, stop=sp, skip_group_check=True))
                       for i in range(cnt)]
                E("tensor", fns, reads=[f"Vb{par}", f"pts{bpar}"], writes=[okey], stream="pe")
            ob = Ab(ATTN[f"O{par}"] + qt * 128, 128)
            E("vector", lambda e, o=ob, i=ops_[:, 0:128]: e.tensor_copy(o, i),
              reads=[okey], writes=[f"Ob{par}"], stream="dve")

        def emit_ostore(h):
            par = h % 2
            dst = d_oin[h // 2].ap()[(h % 2) * 128:(h % 2) * 128 + 128, :]
            E("scalar", lambda e, o=dst, i=Ab(ATTN[f"O{par}"], 4096): e.dma_start(out=o, in_=i),
              reads=[f"Ob{par}"], writes=[f"d_oin{h // 2}"], stream=f"q_do{h // 2}", inc=16)
            if h % 2 == 1 and "cc" in phases:
                hh = h // 2
                E("gpsimd", lambda e, i=d_oin[hh].ap().opt(), o=d_oout[hh].ap().opt():
                  e.collective_compute("AllGather", ALU.bypass, replica_groups=GROUPS,
                                       ins=[i], outs=[o]),
                  reads=[f"d_oin{hh}"], writes=[f"d_oout{hh}"], stream=f"cc_o{hh}")

        from collections import deque
        pend = deque()
        for h in range(HPC):
            emit_loads(h)
            for qt in range(32):
                emit_stats(h, qt)
                pend.append((h, qt))
                if len(pend) > 2:
                    ph, pq = pend.popleft()
                    emit_av(ph, pq)
                    if pq == 31:
                        emit_ostore(ph)
        while pend:
            ph, pq = pend.popleft()
            emit_av(ph, pq)
            if pq == 31:
                emit_ostore(ph)

        tk.fence()
        if "wo" not in phases:
            continue
        nxt = [x for x in lids if x > l]
        if nxt and "ffn" in phases:
            emit_ffn_prefetch(nxt[0])
        for half, wnm in enumerate(("woA", "woB")):
            ol = Ab(WO["OLA" if half == 0 else "OLB"], 8192)
            dram = d_oout[half]

            def dynload(e, dram=dram, dst=sb3(ol, TC)):
                rank = e.partition_id() % 4
                src = dram.ap()[:, ds(rank * TC, TC)].rearrange("(n p) t -> p n t", p=128)
                return e.dma_start(out=dst, in_=src)

            E("gpsimd", dynload, reads=[f"d_oout{half}"], writes=[f"ol{half}"],
              stream=f"q_ol{half}", inc=16)
            for oc in range(16):
                par = oc % 2
                woc = Ab(WO[f"WOC{par}"], 1024)
                wosrc = dram3(W[l, wnm][:, oc * 128:(oc + 1) * 128])
                E("gpsimd", lambda e, o=sb3(woc, 128), i=wosrc: e.dma_start(out=o, in_=i),
                  writes=[f"woc{par}"], stream=f"q_woc{par}", inc=16)
                fns = [(lambda e, o=(oPA if tc2 == 0 else oPB)[:, 0:512],
                        w=woc[:, dc * 128:(dc + 1) * 128],
                        r=ol[:, dc * 1024 + tc2 * 512: dc * 1024 + tc2 * 512 + 512],
                        st=(dc == 0), sp=(dc == 7):
                        e.matmul(o, w, r, start=st, stop=sp, skip_group_check=True))
                       for dc in range(8) for tc2 in range(2)]
                E("tensor", fns, reads=[f"woc{par}", f"ol{half}"], writes=["oPA", "oPB"], stream="pe")
                for tc2 in range(2):
                    pps, pkey2 = (oPA, "oPA") if tc2 == 0 else (oPB, "oPB")
                    xsl = x_sb[:, oc * TC + tc2 * 512: oc * TC + tc2 * 512 + 512]
                    E("vector", lambda e, o=xsl, i=pps[:, 0:512]: e.tensor_add(o, o, i),
                      reads=[pkey2, f"x{oc}"], writes=[f"x{oc}"], stream="dve")

    E("sync", lambda e, o=dram3(xout), i=sb3(x_sb[:, 0:DCN * TC], TC): e.dma_start(out=o, in_=i),
      reads=[f"x{i}" for i in range(DCN)], writes=["d_out"], stream="q_out", inc=16)
    E("sync", [], reads=["d_out"])

    with stack:
        semstack = contextlib.ExitStack()
        with semstack:
            sems.update({s: semstack.enter_context(nc.semaphore(f"sem_{s}"))
                         for s in tk.cnt})
            with nc.Block() as block:
                def runner(ename):
                    def run(eng):
                        for waits, fns, stream, inc in tk.prog[ename]:
                            for s in sorted(waits):
                                eng.wait_ge(sems[s], waits[s])
                            inst = None
                            for f in fns:
                                inst = f(eng)
                            if stream is not None and inst is not None:
                                inst.then_inc(sems[stream], inc)
                    return run
                block.gpsimd(runner("gpsimd"))
                block.scalar(runner("scalar"))
                block.tensor(runner("tensor"))
                block.vector(runner("vector"))
                block.sync(runner("sync"))
    return nc



def _prep(inputs, lids=(0, 1)):
    f32 = np.float32
    X = np.asarray(inputs["L0_attn_X"], dtype=f32)

    i64 = np.arange(64)
    theta = (10000.0 ** (-i64 / 64.0)).astype(np.float64)
    ang = np.outer(theta, np.arange(T).astype(np.float64))
    cos2 = np.concatenate([np.cos(ang), np.cos(ang)], axis=0).astype(f32)
    sin2 = np.concatenate([np.sin(ang), np.sin(ang)], axis=0).astype(f32)
    ident = np.eye(128, dtype=ml_dtypes.bfloat16)
    rmat = np.zeros((128, 128), dtype=f32)
    for i in range(64):
        rmat[64 + i, i] = -1.0
        rmat[i, 64 + i] = 1.0
    rmat = rmat.astype(ml_dtypes.bfloat16)
    tri = np.where(np.arange(128)[None, :] <= np.arange(128)[:, None], 0.0, NEG).astype(f32)

    perm = np.zeros(D, dtype=np.int64)
    for hh in range(NH):
        base = hh * 128
        perm[base:base + 64] = base + 2 * np.arange(64)
        perm[base + 64:base + 128] = base + 2 * np.arange(64) + 1

    def woparts(Wo):
        WoT = np.ascontiguousarray(Wo.T.astype(f32))
        headsA = [4 * r + k for r in range(4) for k in (0, 1)]
        headsB = [4 * r + k for r in range(4) for k in (2, 3)]
        rowsA = np.concatenate([np.arange(hh * 128, hh * 128 + 128) for hh in headsA])
        rowsB = np.concatenate([np.arange(hh * 128, hh * 128 + 128) for hh in headsB])
        return np.ascontiguousarray(WoT[rowsA]), np.ascontiguousarray(WoT[rowsB])

    shared = {"cos2": cos2, "sin2": sin2, "ident": ident, "rmat": rmat, "tri": tri}
    per_layer = {}
    for l in lids:
        if l == 0:
            Wq = np.asarray(inputs["L0_attn_Wq"], f32)[perm]
            Wk = np.asarray(inputs["L0_attn_Wk"], f32)[perm]
            Wv = np.asarray(inputs["L0_attn_Wv"], f32)
            woA, woB = woparts(np.asarray(inputs["L0_attn_Wo"], f32))
            g = np.ascontiguousarray(np.asarray(inputs["L0_ffn_w_gate"], f32).T)
            u = np.ascontiguousarray(np.asarray(inputs["L0_ffn_w_up"], f32).T)
            d = np.ascontiguousarray(np.asarray(inputs["L0_ffn_w_down"], f32).T)
            per_layer[l] = dict(Wq=Wq, Wk=Wk, Wv=Wv, woA=woA, woB=woB, g=g, u=u, d=d)
        else:
            Wq = np.asarray(inputs["L1_attn_Wq"], f32)
            Wk = np.asarray(inputs["L1_attn_Wk"], f32)
            Wv = np.asarray(inputs["L1_attn_Wv"], f32)
            woA, woB = woparts(np.asarray(inputs["L1_attn_Wo"], f32))
            g = np.ascontiguousarray(np.asarray(inputs["L1_ffn_gate_w"], f32).T)
            u = np.ascontiguousarray(np.asarray(inputs["L1_ffn_up_w"], f32).T)
            d = np.ascontiguousarray(np.asarray(inputs["L1_ffn_down_w"], f32).T)
            nht = LCFG[1]["HF"] // 128
            gs = np.ascontiguousarray(np.asarray(inputs["L1_ffn_gate_s"], f32).reshape(nht, 128).T)
            us = np.ascontiguousarray(np.asarray(inputs["L1_ffn_up_s"], f32).reshape(nht, 128).T)
            dsl = np.ascontiguousarray(np.asarray(inputs["L1_ffn_down_s"], f32).reshape(16, 128).T)
            per_layer[l] = dict(Wq=Wq, Wk=Wk, Wv=Wv, woA=woA, woB=woB, g=g, u=u, d=d,
                                gs=gs, us=us, ds=dsl)

    in_maps = []
    for c in range(8):
        b, r = c // 4, c % 4
        m = dict(shared)
        m["xin"] = np.ascontiguousarray(X[b, r * TC:(r + 1) * TC, :].T)
        for l in lids:
            pl = per_layer[l]
            hs = 4 * r
            m[f"wq{l}"] = np.ascontiguousarray(pl["Wq"][hs * 128:(hs + 4) * 128].T)
            m[f"wk{l}"] = np.ascontiguousarray(pl["Wk"][hs * 128:(hs + 4) * 128].T)
            m[f"wv{l}"] = np.ascontiguousarray(pl["Wv"][hs * 128:(hs + 4) * 128].T)
            m[f"woA{l}"] = pl["woA"]
            m[f"woB{l}"] = pl["woB"]
            m[f"g{l}"] = pl["g"]
            m[f"u{l}"] = pl["u"]
            m[f"d{l}"] = pl["d"]
            if LCFG[l]["scales"]:
                m[f"gs{l}"] = pl["gs"]
                m[f"us{l}"] = pl["us"]
                m[f"ds{l}"] = pl["ds"]
        in_maps.append(m)
    return in_maps


def run(inputs, lids=(0, 1), phases=("ln", "ffn", "proj", "attn", "wo", "cc")):
    in_maps = _prep(inputs, lids)
    nc = build_nc(lids, phases)
    res = run_bass_kernel_spmd(nc, in_maps, core_ids=list(range(8)))
    out = np.empty((B, T, D), dtype=np.float32)
    for c in range(8):
        b, r = c // 4, c % 4
        out[b, r * TC:(r + 1) * TC, :] = res.results[c]["xout"].T
    return out


def kernel(**inputs):
    return run(inputs, lids=(0, 1))


# version 06
# speedup vs baseline: 1.2403x; 1.2403x over previous
import numpy as np
import ml_dtypes
import concourse.bass as bass
import concourse.mybir as mybir
from concourse.bass import ds
from concourse.bass_utils import run_bass_kernel_spmd
import concourse.bass_utils as _bu

if not getattr(_bu, "_ldw_patched", False):
    _orig_run_command = _bu.run_command

    def _run_command_ldw(argv, **kwargs):
        argv = [a.replace("--enable-ldw-opt=false", "--enable-ldw-opt=true")
                if isinstance(a, str) else a for a in argv]
        return _orig_run_command(argv, **kwargs)

    _bu.run_command = _run_command_ldw
    _bu._ldw_patched = True

F32 = mybir.dt.float32
BF16 = mybir.dt.bfloat16
AF = mybir.ActivationFunctionType
ALU = mybir.AluOpType
AX = mybir.AxisListType

B, T, D = 2, 4096, 2048
NH, DH = 16, 128
TC = 1024
HPC = 4
DCN = 16
SCALE = float(1.0 / np.sqrt(128.0))
NEG = -1.0e30
EPS = 1e-5
GROUPS = [[0, 1, 2, 3], [4, 5, 6, 7]]

LCFG = [
    dict(idx=0, HF=8192, scales=False),
    dict(idx=1, HF=5632, scales=True),
]

PROJ = dict(XNF0=0, XNF1=8192, WQ=16384, WK=24576, WV=32768,
            QSB0=40960, QSB1=41472, ROUT0=41984, ROUT1=42496,
            VSB0=43008, VSB1=43520)
ATTN = dict(K0=0, K1=4096, V0=8192, V1=12288, Q0=16384, Q1=20480,
            P0=24576, P1=28672, O0=32768, O1=36864, PTS0=40960, PTS1=41984)
NFPF = 6
FFN = dict(HFM=0, DW0=16384, DW1=43008,
           **{f"GW{i}": 18432 + i * 2048 for i in range(NFPF)},
           **{f"UW{i}": 30720 + i * 2048 for i in range(NFPF)})
WO = dict(OLA=0, OLB=8192, WOC0=16384, WOC1=17408)
ATTN["DG0"] = 43008
ATTN["DG1"] = 43520
AFO = dict(T1=0, T2=512, COS0=1024, COS1=1536, SIN0=2048, SIN1=2560,
           GSF0=0, GSF1=512, USF0=1024, USF1=1536, DTMP=2048)
SS_M, SS_CM, SS_BB, SS_S, SS_DD, SS_F, SS_TOT, SS_RR = 0, 4, 5, 8, 12, 16, 20, 21


def cdiv(a, b):
    return (a + b - 1) // b


class TK:

    ENGINES = ("gpsimd", "scalar", "tensor", "vector", "sync")

    def __init__(self):
        self.cnt = {}
        self.wm = {e: {} for e in self.ENGINES}
        self.bufs = {}
        self.prog = {e: [] for e in self.ENGINES}
        self.pending_fence = {e: None for e in self.ENGINES}

    def fence(self):
        snap = {s: c for s, c in self.cnt.items() if c and not s.startswith("cc_")}
        for e in self.ENGINES:
            self.pending_fence[e] = dict(snap)

    def emit(self, eng, fns, reads=(), writes=(), stream=None, inc=1):
        if not isinstance(fns, (list, tuple)):
            fns = [fns]
        need = {}
        pf = self.pending_fence[eng]
        if pf:
            for s, c in pf.items():
                need[s] = max(need.get(s, 0), c)
            self.pending_fence[eng] = None
        for k in reads:
            b = self.bufs.get(k)
            if b and b["w"] is not None:
                s, c = b["w"]
                need[s] = max(need.get(s, 0), c)
        for k in writes:
            b = self.bufs.get(k)
            if b:
                if b["w"] is not None:
                    s, c = b["w"]
                    need[s] = max(need.get(s, 0), c)
                for s, c in b["r"].items():
                    need[s] = max(need.get(s, 0), c)
        wm = self.wm[eng]
        waits = {}
        for s, c in need.items():
            if c > wm.get(s, 0):
                waits[s] = c
                wm[s] = c
        mc = None
        if fns and stream is not None:
            self.cnt[stream] = self.cnt.get(stream, 0) + inc
            mc = (stream, self.cnt[stream])
        self.prog[eng].append((waits, fns, stream if mc else None, inc))
        for k in reads:
            b = self.bufs.setdefault(k, {"w": None, "r": {}})
            if mc:
                b["r"][mc[0]] = mc[1]
        for k in writes:
            assert mc is not None, f"writer of {k} must signal"
            self.bufs[k] = {"w": mc, "r": {}}


def build_nc(lids=(0, 1), phases=("ln", "ffn", "proj", "attn", "wo", "cc")):
    nc = bass.Bass("TRN2", target_bir_lowering=False)
    tk = TK()

    xin = nc.dram_tensor("xin", [D, TC], F32, kind="ExternalInput").ap()
    xout = nc.dram_tensor("xout", [D, TC], F32, kind="ExternalOutput").ap()
    cos2 = nc.dram_tensor("cos2", [128, T], F32, kind="ExternalInput").ap()
    sin2 = nc.dram_tensor("sin2", [128, T], F32, kind="ExternalInput").ap()
    ident_in = nc.dram_tensor("ident", [128, 128], BF16, kind="ExternalInput").ap()
    rmat_in = nc.dram_tensor("rmat", [128, 128], BF16, kind="ExternalInput").ap()
    tri_in = nc.dram_tensor("tri", [128, 128], F32, kind="ExternalInput").ap()
    W = {}
    for l in lids:
        cfg = LCFG[l]
        W[l, "wq"] = nc.dram_tensor(f"wq{l}", [D, 512], F32, kind="ExternalInput").ap()
        W[l, "wk"] = nc.dram_tensor(f"wk{l}", [D, 512], F32, kind="ExternalInput").ap()
        W[l, "wv"] = nc.dram_tensor(f"wv{l}", [D, 512], F32, kind="ExternalInput").ap()
        W[l, "woA"] = nc.dram_tensor(f"woA{l}", [1024, D], F32, kind="ExternalInput").ap()
        W[l, "woB"] = nc.dram_tensor(f"woB{l}", [1024, D], F32, kind="ExternalInput").ap()
        W[l, "g"] = nc.dram_tensor(f"g{l}", [D, cfg["HF"]], F32, kind="ExternalInput").ap()
        W[l, "u"] = nc.dram_tensor(f"u{l}", [D, cfg["HF"]], F32, kind="ExternalInput").ap()
        W[l, "d"] = nc.dram_tensor(f"d{l}", [cfg["HF"], D], F32, kind="ExternalInput").ap()
        if cfg["scales"]:
            nht = cfg["HF"] // 128
            W[l, "gs"] = nc.dram_tensor(f"gs{l}", [128, nht], F32, kind="ExternalInput").ap()
            W[l, "us"] = nc.dram_tensor(f"us{l}", [128, nht], F32, kind="ExternalInput").ap()
            W[l, "ds"] = nc.dram_tensor(f"ds{l}", [128, 16], F32, kind="ExternalInput").ap()

    d_xnin = nc.dram_tensor("d_xnin", [D, TC], BF16, kind="Internal")
    d_xnfull = nc.dram_tensor("d_xnfull", [4 * D, TC], BF16, kind="Internal")
    d_qk = nc.dram_tensor("d_qk", [2, HPC, 128, T], BF16, kind="Internal")
    d_v = nc.dram_tensor("d_v", [HPC, 128, 32, 128], BF16, kind="Internal")
    d_oin = [nc.dram_tensor(f"d_o{h}in", [256, T], BF16, kind="Internal") for h in range(2)]
    d_oout = [nc.dram_tensor(f"d_o{h}out", [1024, T], BF16, kind="Internal") for h in range(2)]

    import contextlib
    stack = contextlib.ExitStack()
    sb = lambda name, shape, dt: stack.enter_context(nc.sbuf_tensor(name, shape, dt))
    ps = lambda name, shape, dt: stack.enter_context(nc.psum_tensor(name, shape, dt))

    x_sb = sb("x_sb", [128, DCN * TC], F32)
    xn_sb = sb("xn_sb", [128, DCN * TC], BF16)
    AB = sb("AB", [128, 45056], BF16)
    AFT = sb("AFT", [128, 3072], F32)
    ln_xbf = sb("ln_xbf", [128, 1024], BF16)
    ln_x2 = sb("ln_x2", [128, 1024], BF16)
    SS = sb("SS", [128, 32], F32)
    ident_sb = sb("ident_sb", [128, 128], BF16)
    rmat_sb = sb("rmat_sb", [128, 128], BF16)
    tri_sb = sb("tri_sb", [128, 128], F32)
    onesb = sb("onesb", [128, 1], BF16)
    onesf = sb("onesf", [1, 128], F32)
    sc_sb = {}
    for l in lids:
        if LCFG[l]["scales"]:
            nht = LCFG[l]["HF"] // 128
            sc_sb[l, "gs"] = sb(f"gs_sb{l}", [128, nht], F32)
            sc_sb[l, "us"] = sb(f"us_sb{l}", [128, nht], F32)
            sc_sb[l, "ds"] = sb(f"ds_sb{l}", [128, 16], F32)

    sA = ps("sA", [128, 1024], F32)
    sB = ps("sB", [128, 1024], F32)
    ptA = ps("ptA", [128, 512], F32)
    ptB = ps("ptB", [128, 512], F32)
    oPA = ps("oPA", [128, 512], F32)
    oPB = ps("oPB", [128, 512], F32)

    sems = {}

    def Ab(off, n):
        return AB[:, off:off + n]

    def Af(off, n):
        return AFT[:, off:off + n]

    E = tk.emit

    def dram3(ap2, p=128):
        return ap2.rearrange("(n p) t -> p n t", p=p)

    def sb3(ap2, inner):
        return ap2.rearrange("p (n t) -> p n t", t=inner)

    E("vector", lambda e: e.memset(onesb[:], 1.0), writes=["onesb"], stream="dve")
    E("vector", lambda e: e.memset(onesf[:], 1.0), writes=["onesf"], stream="dve")
    E("sync", lambda e: e.dma_start(out=ident_sb[:], in_=ident_in),
      writes=["ident"], stream="q_ident", inc=16)
    E("sync", lambda e: e.dma_start(out=rmat_sb[:], in_=rmat_in),
      writes=["rmat"], stream="q_rmat", inc=16)
    E("sync", lambda e: e.dma_start(out=tri_sb[:], in_=tri_in),
      writes=["tri"], stream="q_tri", inc=16)
    E("sync", lambda e, a=dram3(xin), o=sb3(x_sb[:, 0:DCN * TC], TC): e.dma_start(out=o, in_=a),
      writes=[f"x{i}" for i in range(DCN)], stream="q_xin", inc=16)
    for l in lids:
        if LCFG[l]["scales"]:
            for nm in ("gs", "us", "ds"):
                E("sync", lambda e, o=sc_sb[l, nm], i=W[l, nm]: e.dma_start(out=o[:], in_=i),
                  writes=[f"{nm}{l}"], stream=f"q_{nm}{l}", inc=16)

    def emit_ffn_prefetch(l):
        for ht in range(min(NFPF, LCFG[l]["HF"] // 128)):
            gw = Ab(FFN[f"GW{ht}"], 2048)
            uw = Ab(FFN[f"UW{ht}"], 2048)
            gsrc = dram3(W[l, "g"][:, ht * 128:(ht + 1) * 128])
            usrc = dram3(W[l, "u"][:, ht * 128:(ht + 1) * 128])
            E("gpsimd", lambda e, o=sb3(gw, 128), i=gsrc: e.dma_start(out=o, in_=i),
              writes=[f"gw{ht}"], stream=f"q_gw{ht}", inc=16)
            E("gpsimd", lambda e, o=sb3(uw, 128), i=usrc: e.dma_start(out=o, in_=i),
              writes=[f"uw{ht}"], stream=f"q_uw{ht}", inc=16)

    emit_ffn_prefetch(lids[0])

    for l in lids:
        cfg = LCFG[l]
        HF = cfg["HF"]
        HT = HF // 128
        HTQ = HT // 4

        if "ln" not in phases:
            continue
        for tc2 in range(2):
            for dc in range(DCN):
                par = dc % 2
                xbf = ln_xbf[:, par * 512:(par + 1) * 512]
                x2bf = ln_x2[:, par * 512:(par + 1) * 512]
                xt = x_sb[:, dc * TC + tc2 * 512: dc * TC + tc2 * 512 + 512]
                E("vector", lambda e, o=xbf, i=xt: e.tensor_copy(o, i),
                  reads=[f"x{dc}"], writes=[f"lnxbf{par}"], stream="dve")
                E("scalar", lambda e, o=x2bf, i=xt: e.activation(o, i, AF.Square),
                  reads=[f"x{dc}"], writes=[f"lnx2bf{par}"], stream="act")
                E("tensor", lambda e, o=oPA[0:1, 0:512], i=xbf, st=(dc == 0), sp=(dc == DCN - 1):
                  e.matmul(o, onesb[:], i, start=st, stop=sp),
                  reads=[f"lnxbf{par}", "onesb"], writes=["oPA"], stream="pe")
                E("tensor", lambda e, o=oPB[0:1, 0:512], i=x2bf, st=(dc == 0), sp=(dc == DCN - 1):
                  e.matmul(o, onesb[:], i, start=st, stop=sp),
                  reads=[f"lnx2bf{par}", "onesb"], writes=["oPB"], stream="pe")
            msf = AFT[0:1, 0:512]
            v1t = AFT[0:1, 512:1024]
            m2t = AFT[0:1, 1024:1536]
            lnv = AFT[0:1, 1024:1536]
            rst = AFT[0:1, 1536:2048]
            E("vector", lambda e, o=msf: e.tensor_scalar_mul(o, oPA[0:1, 0:512], 1.0 / 2048.0),
              reads=["oPA"], writes=["ln0"], stream="dve")
            E("vector", lambda e, o=v1t: e.tensor_scalar(o, oPB[0:1, 0:512], 1.0 / 2048.0, EPS,
                                                         ALU.mult, ALU.add),
              reads=["oPB"], writes=["lnA"], stream="dve")
            E("vector", lambda e, o=m2t, i=msf: e.tensor_mul(o, i, i),
              reads=["ln0"], writes=["lnB"], stream="dve")
            E("vector", lambda e, o=v1t, i=m2t: e.tensor_sub(o, o, i),
              reads=["lnA", "lnB"], writes=["lnA"], stream="dve")
            E("tensor", lambda e, i=msf: e.matmul(sA[:, 0:512], onesf[:], i, start=True, stop=True),
              reads=["ln0", "onesf"], writes=["sA0"], stream="pe")
            E("scalar", lambda e, o=lnv, i=v1t: e.activation(o, i, AF.Ln),
              reads=["lnA"], writes=["lnB"], stream="act")
            E("scalar", lambda e, o=rst, i=lnv: e.activation(o, i, AF.Exp, scale=-0.5),
              reads=["lnB"], writes=["lnC"], stream="act")
            E("tensor", lambda e, i=rst: e.matmul(sB[:, 0:512], onesf[:], i, start=True, stop=True),
              reads=["lnC", "onesf"], writes=["sB0"], stream="pe")
            for dc in range(DCN):
                xt = x_sb[:, dc * TC + tc2 * 512: dc * TC + tc2 * 512 + 512]
                xnt = xn_sb[:, dc * TC + tc2 * 512: dc * TC + tc2 * 512 + 512]
                E("vector", lambda e, o=Af(2048, 512), i=xt: e.tensor_sub(o, i, sA[:, 0:512]),
                  reads=[f"x{dc}", "sA0"], writes=["ln_xt"], stream="dve")
                E("vector", lambda e, o=xnt, i=Af(2048, 512): e.tensor_mul(o, i, sB[:, 0:512]),
                  reads=["ln_xt", "sB0"], writes=["xn"], stream="dve")
        E("sync", lambda e, o=dram3(d_xnin.ap()), i=sb3(xn_sb[:, 0:DCN * TC], TC): e.dma_start(out=o, in_=i),
          reads=["xn"], writes=["d_xnin"], stream="q_xnin", inc=16)
        if "cc" in phases:
            E("gpsimd", lambda e: e.collective_compute(
                "AllGather", ALU.bypass, replica_groups=GROUPS,
                ins=[d_xnin.ap().opt()], outs=[d_xnfull.ap().opt()]),
              reads=["d_xnin"], writes=["d_xnfull"], stream="cc_xn")

        tk.fence()
        if "ffn" not in phases:
            continue
        for qtr in range(4):
            for j in range(HTQ):
                ht = qtr * HTQ + j
                par = ht % NFPF
                gw = Ab(FFN[f"GW{par}"], 2048)
                uw = Ab(FFN[f"UW{par}"], 2048)
                if ht >= NFPF:
                    gsrc = dram3(W[l, "g"][:, ht * 128:(ht + 1) * 128])
                    usrc = dram3(W[l, "u"][:, ht * 128:(ht + 1) * 128])
                    E("gpsimd", lambda e, o=sb3(gw, 128), i=gsrc: e.dma_start(out=o, in_=i),
                      writes=[f"gw{par}"], stream=f"q_gw{par}", inc=16)
                    E("gpsimd", lambda e, o=sb3(uw, 128), i=usrc: e.dma_start(out=o, in_=i),
                      writes=[f"uw{par}"], stream=f"q_uw{par}", inc=16)
                for tc2 in range(2):
                    gps = sA[:, tc2 * 512:(tc2 + 1) * 512]
                    ups = sB[:, tc2 * 512:(tc2 + 1) * 512]
                    gk, uk = f"sA{tc2}", f"sB{tc2}"
                    fns = [(lambda e, o=gps, w=gw[:, dc * 128:(dc + 1) * 128],
                            r=xn_sb[:, dc * TC + tc2 * 512: dc * TC + tc2 * 512 + 512],
                            st=(dc == 0), sp=(dc == DCN - 1):
                            e.matmul(o, w, r, start=st, stop=sp)) for dc in range(DCN)]
                    E("tensor", fns, reads=[f"gw{par}", "xn"], writes=[gk], stream="pe")
                    fns = [(lambda e, o=ups, w=uw[:, dc * 128:(dc + 1) * 128],
                            r=xn_sb[:, dc * TC + tc2 * 512: dc * TC + tc2 * 512 + 512],
                            st=(dc == 0), sp=(dc == DCN - 1):
                            e.matmul(o, w, r, start=st, stop=sp)) for dc in range(DCN)]
                    E("tensor", fns, reads=[f"uw{par}", "xn"], writes=[uk], stream="pe")
                    gsf = Af(AFO[f"GSF{tc2}"], 512)
                    usf = Af(AFO[f"USF{tc2}"], 512)
                    if cfg["scales"]:
                        gsc = sc_sb[l, "gs"][:, ht:ht + 1]
                        usc = sc_sb[l, "us"][:, ht:ht + 1]
                        E("scalar", lambda e, o=gsf, i=gps, s=gsc: e.activation(o, i, AF.Silu, scale=s),
                          reads=[gk, f"gs{l}"], writes=[f"gsf{tc2}"], stream="act")
                        E("scalar", lambda e, o=usf, i=ups, s=usc: e.activation(o, i, AF.Copy, scale=s),
                          reads=[uk, f"us{l}"], writes=[f"usf{tc2}"], stream="act")
                    else:
                        E("scalar", lambda e, o=gsf, i=gps: e.activation(o, i, AF.Silu),
                          reads=[gk], writes=[f"gsf{tc2}"], stream="act")
                        E("scalar", lambda e, o=usf, i=ups: e.activation(o, i, AF.Copy),
                          reads=[uk], writes=[f"usf{tc2}"], stream="act")
                    hdst = Ab(FFN["HFM"] + j * 1024 + tc2 * 512, 512)
                    E("vector", lambda e, o=hdst, a=gsf, b=usf: e.tensor_mul(o, a, b),
                      reads=[f"gsf{tc2}", f"usf{tc2}"], writes=["hfm"], stream="dve")
            for oc in range(16):
                par = oc % 2
                dw = Ab(FFN[f"DW{par}"], HTQ * 128)
                dsrc = dram3(W[l, "d"][qtr * HTQ * 128:(qtr + 1) * HTQ * 128,
                                       oc * 128:(oc + 1) * 128])
                E("gpsimd", lambda e, o=sb3(dw, 128), i=dsrc: e.dma_start(out=o, in_=i),
                  writes=[f"dw{par}"], stream=f"q_dw{par}", inc=16)
                for tc2 in range(2):
                    dps, dkey = (oPA, "oPA") if (oc * 2 + tc2) % 2 == 0 else (oPB, "oPB")
                    fns = [(lambda e, o=dps[:, 0:512], w=dw[:, hl * 128:(hl + 1) * 128],
                            r=Ab(FFN["HFM"] + hl * 1024 + tc2 * 512, 512),
                            st=(hl == 0), sp=(hl == HTQ - 1):
                            e.matmul(o, w, r, start=st, stop=sp)) for hl in range(HTQ)]
                    E("tensor", fns, reads=[f"dw{par}", "hfm"], writes=[dkey], stream="pe")
                    xsl = x_sb[:, oc * TC + tc2 * 512: oc * TC + tc2 * 512 + 512]
                    if cfg["scales"]:
                        dtmp = Af(AFO["DTMP"], 512)
                        dsc = sc_sb[l, "ds"][:, oc:oc + 1]
                        E("vector", lambda e, o=dtmp, i=dps[:, 0:512], s=dsc: e.tensor_scalar_mul(o, i, s),
                          reads=[dkey, f"ds{l}"], writes=["dtmp"], stream="dve")
                        E("vector", lambda e, o=xsl, i=dtmp: e.tensor_add(o, o, i),
                          reads=["dtmp", f"x{oc}"], writes=[f"x{oc}"], stream="dve")
                    else:
                        E("vector", lambda e, o=xsl, i=dps[:, 0:512]: e.tensor_add(o, o, i),
                          reads=[dkey, f"x{oc}"], writes=[f"x{oc}"], stream="dve")

        tk.fence()
        if "proj" not in phases:
            continue
        plvl = 4
        for p in phases:
            if p.startswith("plvl"):
                plvl = int(p[4:])
        for nm, off in (("wq", PROJ["WQ"]), ("wk", PROJ["WK"]), ("wv", PROJ["WV"])):
            wsrc = dram3(W[l, nm])
            E("gpsimd", lambda e, o=sb3(Ab(off, 8192), 512), i=wsrc: e.dma_start(out=o, in_=i),
              writes=[nm], stream=f"q_{nm}", inc=16)
        for tc in range(8):
            par = tc % 2
            rr, c0 = tc // 2, (tc % 2) * 512
            xnf = Ab(PROJ[f"XNF{par}"], 8192)
            xnfsrc = dram3(d_xnfull.ap()[2048 * rr:2048 * (rr + 1), c0:c0 + 512])
            E("sync", lambda e, o=sb3(xnf, 512), i=xnfsrc: e.dma_start(out=o, in_=i),
              reads=["d_xnfull"], writes=[f"xnf{par}"], stream=f"q_xnf{par}", inc=16)
            cosb = Af(AFO[f"COS{par}"], 512)
            sinb = Af(AFO[f"SIN{par}"], 512)
            E("sync", lambda e, o=cosb, i=cos2[:, tc * 512:(tc + 1) * 512]: e.dma_start(out=o, in_=i),
              writes=[f"cos{par}"], stream=f"q_cos{par}", inc=16)
            E("sync", lambda e, o=sinb, i=sin2[:, tc * 512:(tc + 1) * 512]: e.dma_start(out=o, in_=i),
              writes=[f"sin{par}"], stream=f"q_sin{par}", inc=16)
            for h in range(HPC):
                for s, (woff, wkey) in enumerate(((PROJ["WQ"], "wq"), (PROJ["WK"], "wk"))):
                    mps, mk = (sA[:, 0:512], "sA0") if s == 0 else (sB[:, 0:512], "sB0")
                    rps, rk = (sA[:, 512:1024], "sA1") if s == 0 else (sB[:, 512:1024], "sB1")
                    fns = [(lambda e, o=mps, w=Ab(woff + dc * 512 + h * 128, 128),
                            r=Ab(PROJ[f"XNF{par}"] + dc * 512, 512),
                            st=(dc == 0), sp=(dc == DCN - 1):
                            e.matmul(o, w, r, start=st, stop=sp)) for dc in range(DCN)]
                    E("tensor", fns, reads=[wkey, f"xnf{par}"], writes=[mk], stream="pe")
                    qsb = Ab(PROJ[f"QSB{s}"], 512)
                    E("scalar", lambda e, o=qsb, i=mps: e.activation(o, i, AF.Copy),
                      reads=[mk], writes=[f"qsb{s}", mk], stream="act")
                    if plvl < 2:
                        continue
                    E("tensor", lambda e, o=rps, i=qsb: e.matmul(o, rmat_sb[:], i, start=True, stop=True),
                      reads=[f"qsb{s}", "rmat"], writes=[rk], stream="pe")
                    if plvl < 3:
                        continue
                    t1 = Af(AFO["T1"], 512)
                    t2 = Af(AFO["T2"], 512)
                    rout = Ab(PROJ[f"ROUT{s}"], 512)
                    E("vector", lambda e, o=t1, i=mps, c=cosb: e.tensor_mul(o, i, c),
                      reads=[mk, f"cos{par}"], writes=["t1"], stream="dve")
                    E("vector", lambda e, o=t2, i=rps, c=sinb: e.tensor_mul(o, i, c),
                      reads=[rk, f"sin{par}"], writes=["t2"], stream="dve")
                    E("vector", lambda e, o=rout, a=t1, b=t2: e.tensor_add(o, a, b),
                      reads=["t1", "t2"], writes=[f"rout{s}"], stream="dve")
                    dst = d_qk.ap()[s, h, :, tc * 512:(tc + 1) * 512]
                    E("sync", lambda e, o=dst, i=rout: e.dma_start(out=o, in_=i),
                      reads=[f"rout{s}"], writes=["d_qk"], stream="q_dqk", inc=16)
            if plvl < 4:
                continue
            for tt in range(4):
                vps, vk = (oPA, "oPA") if tt % 2 == 0 else (oPB, "oPB")
                fns = [(lambda e, o=vps[:, 0:512],
                        w=Ab(PROJ[f"XNF{par}"] + dc * 512 + tt * 128, 128),
                        r=Ab(PROJ["WV"] + dc * 512, 512),
                        st=(dc == 0), sp=(dc == DCN - 1):
                        e.matmul(o, w, r, start=st, stop=sp)) for dc in range(DCN)]
                E("tensor", fns, reads=["wv", f"xnf{par}"], writes=[vk], stream="pe")
                vsb = Ab(PROJ[f"VSB{tt % 2}"], 512)
                E("vector", lambda e, o=vsb, i=vps[:, 0:512]: e.tensor_copy(o, i),
                  reads=[vk], writes=[f"vsb{tt % 2}"], stream="dve")
                TT = tc * 4 + tt
                dst = d_v.ap()[:, :, TT, :].transpose([1, 0, 2])
                E("sync", lambda e, o=dst, i=vsb.rearrange("p (h j) -> p h j", h=4):
                  e.dma_start(out=o, in_=i),
                  reads=[f"vsb{tt % 2}"], writes=["d_v"], stream="q_dv", inc=16)

        tk.fence()
        if "attn" not in phases:
            continue
        av_ctr = [0]

        def emit_loads(h):
            par = h % 2
            kb = Ab(ATTN[f"K{par}"], 4096)
            vb = Ab(ATTN[f"V{par}"], 4096)
            qb = Ab(ATTN[f"Q{par}"], 4096)
            E("sync", lambda e, o=kb, i=d_qk.ap()[1, h]: e.dma_start(out=o, in_=i),
              reads=["d_qk"], writes=[f"Kb{par}"], stream=f"q_K{par}", inc=16)
            E("sync", lambda e, o=qb, i=d_qk.ap()[0, h]: e.dma_start(out=o, in_=i),
              reads=["d_qk"], writes=[f"Qb{par}"], stream=f"q_Q{par}", inc=16)
            E("sync", lambda e, o=vb, i=d_v.ap()[h].rearrange("p tt j -> p (tt j)"):
              e.dma_start(out=o, in_=i),
              reads=["d_v"], writes=[f"Vb{par}"], stream=f"q_V{par}", inc=16)

        def emit_stats(h, qt):
            par = h % 2
            kb = Ab(ATTN[f"K{par}"], 4096)
            qb = Ab(ATTN[f"Q{par}"], 4096)
            Ecols = (qt + 1) * 128
            Nc = cdiv(Ecols, 1024)
            pP = Ab(ATTN[f"P{qt % 2}"], 4096)
            pkey = f"P{qt % 2}"
            for c in range(Nc):
                lo = c * 1024
                hi = min(Ecols, lo + 1024)
                w = hi - lo
                sps, skeys = (sA, ["sA0", "sA1"]) if c % 2 == 0 else (sB, ["sB0", "sB1"])
                fns = []
                for b0 in range(0, w, 512):
                    bw = min(512, w - b0)
                    fns.append(lambda e, o=sps[:, b0:b0 + bw],
                               q=qb[:, qt * 128:(qt + 1) * 128],
                               k=kb[:, lo + b0:lo + b0 + bw]:
                               e.matmul(o, q, k, start=True, stop=True))
                E("tensor", fns, reads=[f"Qb{par}", f"Kb{par}"], writes=skeys, stream="pe")
                if hi == Ecols:
                    E("vector", lambda e, o=sps[:, w - 128:w]: e.tensor_add(o, o, tri_sb[:]),
                      reads=skeys + ["tri"], writes=skeys, stream="dve")
                cm = SS[:, SS_CM:SS_CM + 1]
                E("vector", lambda e, o=cm, i=sps[:, 0:w]: e.reduce_max(o, i, axis=AX.X),
                  reads=skeys, writes=["ss"], stream="dve")
                mc = SS[:, SS_M + c:SS_M + c + 1]
                if c == 0:
                    E("vector", lambda e, o=mc, i=cm: e.tensor_copy(o, i),
                      reads=["ss"], writes=["ss"], stream="dve")
                else:
                    mprev = SS[:, SS_M + c - 1:SS_M + c]
                    E("vector", lambda e, o=mc, a=mprev, b=cm: e.tensor_max(o, a, b),
                      reads=["ss"], writes=["ss"], stream="dve")
                bb = SS[:, SS_BB:SS_BB + 1]
                E("vector", lambda e, o=bb, i=mc: e.tensor_scalar_mul(o, i, -SCALE),
                  reads=["ss"], writes=["ss"], stream="dve")
                sc = SS[:, SS_S + c:SS_S + c + 1]
                E("scalar", lambda e, o=pP[:, lo:hi], i=sps[:, 0:w], b=bb, a=sc:
                  e.activation(o, i, AF.Exp, bias=b, scale=SCALE, accum_out=a),
                  reads=skeys + ["ss"], writes=[pkey, "ss"], stream="act")
            dgb = Ab(ATTN[f"DG{qt % 2}"], 512)
            dgkey = f"dg{qt % 2}"
            rr_ = SS[:, SS_RR:SS_RR + 1]
            if Nc == 1:
                E("vector", lambda e, o=rr_: e.reciprocal(o, SS[:, SS_S:SS_S + 1]),
                  reads=["ss"], writes=["ss"], stream="dve")
                E("vector", lambda e, o=dgb[:, 0:128], s=rr_:
                  e.tensor_scalar_mul(o, ident_sb[:], s),
                  reads=["ss", "ident"], writes=[dgkey], stream="dve")
            else:
                dd = SS[:, SS_DD:SS_DD + Nc]
                E("vector", lambda e, o=dd, i=SS[:, SS_M:SS_M + Nc],
                  s1=SS[:, SS_M + Nc - 1:SS_M + Nc]:
                  e.tensor_scalar(o, i, s1, SCALE, ALU.subtract, ALU.mult),
                  reads=["ss"], writes=["ss"], stream="dve")
                ff = SS[:, SS_F:SS_F + Nc]
                E("scalar", lambda e, o=ff, i=dd: e.activation(o, i, AF.Exp),
                  reads=["ss"], writes=["ss"], stream="act")
                E("vector", lambda e, o=dd, a=ff, b=SS[:, SS_S:SS_S + Nc]: e.tensor_mul(o, a, b),
                  reads=["ss"], writes=["ss"], stream="dve")
                tot = SS[:, SS_TOT:SS_TOT + 1]
                E("vector", lambda e, o=tot, i=dd: e.reduce_sum(o, i, axis=AX.X),
                  reads=["ss"], writes=["ss"], stream="dve")
                E("vector", lambda e, o=rr_, i=tot: e.reciprocal(o, i),
                  reads=["ss"], writes=["ss"], stream="dve")
                E("vector", lambda e, o=ff, s=rr_: e.tensor_scalar_mul(o, o, s),
                  reads=["ss"], writes=["ss"], stream="dve")
                for c in range(Nc):
                    fc = SS[:, SS_F + c:SS_F + c + 1]
                    E("vector", lambda e, o=dgb[:, c * 128:(c + 1) * 128], s=fc:
                      e.tensor_scalar_mul(o, ident_sb[:], s),
                      reads=["ss", "ident"], writes=[dgkey], stream="dve")

        def emit_av(h, qt):
            par = h % 2
            vb = Ab(ATTN[f"V{par}"], 4096)
            Ecols = (qt + 1) * 128
            nkt = Ecols // 128
            pP = Ab(ATTN[f"P{qt % 2}"], 4096)
            pkey = f"P{qt % 2}"
            dgb = Ab(ATTN[f"DG{qt % 2}"], 512)
            dgkey = f"dg{qt % 2}"
            ops_, okey = (oPA, "oPA") if qt % 2 == 0 else (oPB, "oPB")
            for base in range(0, nkt, 4):
                cnt = min(4, nkt - base)
                bpar = av_ctr[0] % 2
                av_ctr[0] += 1
                ptp, ptkey = (ptA, "ptA") if bpar == 0 else (ptB, "ptB")
                dgs = dgb[:, (base // 8) * 128:(base // 8) * 128 + 128]
                fns = [(lambda e, o=ptp[:, i * 128:(i + 1) * 128],
                        i_=pP[:, (base + i) * 128:(base + i + 1) * 128], d=dgs:
                        e.matmul(o, i_, d, start=True, stop=True)) for i in range(cnt)]
                E("tensor", fns, reads=[pkey, dgkey], writes=[ptkey], stream="pe")
                pts = Ab(ATTN[f"PTS{bpar}"], 1024)
                E("scalar", lambda e, o=pts[:, 0:cnt * 128], i=ptp[:, 0:cnt * 128]:
                  e.activation(o, i, AF.Copy),
                  reads=[ptkey], writes=[f"pts{bpar}", ptkey], stream="act")
                fns = [(lambda e, o=ops_[:, 0:128],
                        v=vb[:, (base + i) * 128:(base + i + 1) * 128],
                        r=pts[:, i * 128:(i + 1) * 128],
                        st=(base + i == 0), sp=(base + i == nkt - 1):
                        e.matmul(o, v, r, start=st, stop=sp, skip_group_check=True))
                       for i in range(cnt)]
                E("tensor", fns, reads=[f"Vb{par}", f"pts{bpar}"], writes=[okey], stream="pe")
            ob = Ab(ATTN[f"O{par}"] + qt * 128, 128)
            E("vector", lambda e, o=ob, i=ops_[:, 0:128]: e.tensor_copy(o, i),
              reads=[okey], writes=[f"Ob{par}"], stream="dve")

        def emit_ostore(h):
            par = h % 2
            dst = d_oin[h // 2].ap()[(h % 2) * 128:(h % 2) * 128 + 128, :]
            E("scalar", lambda e, o=dst, i=Ab(ATTN[f"O{par}"], 4096): e.dma_start(out=o, in_=i),
              reads=[f"Ob{par}"], writes=[f"d_oin{h // 2}"], stream=f"q_do{h // 2}", inc=16)
            if h % 2 == 1 and "cc" in phases:
                hh = h // 2
                E("gpsimd", lambda e, i=d_oin[hh].ap().opt(), o=d_oout[hh].ap().opt():
                  e.collective_compute("AllGather", ALU.bypass, replica_groups=GROUPS,
                                       ins=[i], outs=[o]),
                  reads=[f"d_oin{hh}"], writes=[f"d_oout{hh}"], stream=f"cc_o{hh}")

        pending = None
        for h in range(HPC):
            emit_loads(h)
            for qt in range(32):
                emit_stats(h, qt)
                if pending is not None:
                    emit_av(*pending)
                    if pending[1] == 31:
                        emit_ostore(pending[0])
                pending = (h, qt)
        emit_av(*pending)
        emit_ostore(pending[0])

        tk.fence()
        if "wo" not in phases:
            continue
        nxt = [x for x in lids if x > l]
        if nxt and "ffn" in phases:
            emit_ffn_prefetch(nxt[0])
        for half, wnm in enumerate(("woA", "woB")):
            ol = Ab(WO["OLA" if half == 0 else "OLB"], 8192)
            dram = d_oout[half]

            def dynload(e, dram=dram, dst=sb3(ol, TC)):
                rank = e.partition_id() % 4
                src = dram.ap()[:, ds(rank * TC, TC)].rearrange("(n p) t -> p n t", p=128)
                return e.dma_start(out=dst, in_=src)

            E("gpsimd", dynload, reads=[f"d_oout{half}"], writes=[f"ol{half}"],
              stream=f"q_ol{half}", inc=16)
            for oc in range(16):
                par = oc % 2
                woc = Ab(WO[f"WOC{par}"], 1024)
                wosrc = dram3(W[l, wnm][:, oc * 128:(oc + 1) * 128])
                E("gpsimd", lambda e, o=sb3(woc, 128), i=wosrc: e.dma_start(out=o, in_=i),
                  writes=[f"woc{par}"], stream=f"q_woc{par}", inc=16)
                for tc2 in range(2):
                    pps, pkey2 = (oPA, "oPA") if (oc * 2 + tc2) % 2 == 0 else (oPB, "oPB")
                    fns = [(lambda e, o=pps[:, 0:512], w=woc[:, dc * 128:(dc + 1) * 128],
                            r=ol[:, dc * 1024 + tc2 * 512: dc * 1024 + tc2 * 512 + 512],
                            st=(dc == 0), sp=(dc == 7):
                            e.matmul(o, w, r, start=st, stop=sp)) for dc in range(8)]
                    E("tensor", fns, reads=[f"woc{par}", f"ol{half}"], writes=[pkey2], stream="pe")
                    xsl = x_sb[:, oc * TC + tc2 * 512: oc * TC + tc2 * 512 + 512]
                    E("vector", lambda e, o=xsl, i=pps[:, 0:512]: e.tensor_add(o, o, i),
                      reads=[pkey2, f"x{oc}"], writes=[f"x{oc}"], stream="dve")

    E("sync", lambda e, o=dram3(xout), i=sb3(x_sb[:, 0:DCN * TC], TC): e.dma_start(out=o, in_=i),
      reads=[f"x{i}" for i in range(DCN)], writes=["d_out"], stream="q_out", inc=16)
    E("sync", [], reads=["d_out"])

    with stack:
        semstack = contextlib.ExitStack()
        with semstack:
            sems.update({s: semstack.enter_context(nc.semaphore(f"sem_{s}"))
                         for s in tk.cnt})
            with nc.Block() as block:
                def runner(ename):
                    def run(eng):
                        for waits, fns, stream, inc in tk.prog[ename]:
                            for s in sorted(waits):
                                eng.wait_ge(sems[s], waits[s])
                            inst = None
                            for f in fns:
                                inst = f(eng)
                            if stream is not None and inst is not None:
                                inst.then_inc(sems[stream], inc)
                    return run
                block.gpsimd(runner("gpsimd"))
                block.scalar(runner("scalar"))
                block.tensor(runner("tensor"))
                block.vector(runner("vector"))
                block.sync(runner("sync"))
    return nc



def _prep(inputs, lids=(0, 1)):
    f32 = np.float32
    X = np.asarray(inputs["L0_attn_X"], dtype=f32)

    i64 = np.arange(64)
    theta = (10000.0 ** (-i64 / 64.0)).astype(np.float64)
    ang = np.outer(theta, np.arange(T).astype(np.float64))
    cos2 = np.concatenate([np.cos(ang), np.cos(ang)], axis=0).astype(f32)
    sin2 = np.concatenate([np.sin(ang), np.sin(ang)], axis=0).astype(f32)
    ident = np.eye(128, dtype=ml_dtypes.bfloat16)
    rmat = np.zeros((128, 128), dtype=f32)
    for i in range(64):
        rmat[64 + i, i] = -1.0
        rmat[i, 64 + i] = 1.0
    rmat = rmat.astype(ml_dtypes.bfloat16)
    tri = np.where(np.arange(128)[None, :] <= np.arange(128)[:, None], 0.0, NEG).astype(f32)

    perm = np.zeros(D, dtype=np.int64)
    for hh in range(NH):
        base = hh * 128
        perm[base:base + 64] = base + 2 * np.arange(64)
        perm[base + 64:base + 128] = base + 2 * np.arange(64) + 1

    def woparts(Wo):
        WoT = np.ascontiguousarray(Wo.T.astype(f32))
        headsA = [4 * r + k for r in range(4) for k in (0, 1)]
        headsB = [4 * r + k for r in range(4) for k in (2, 3)]
        rowsA = np.concatenate([np.arange(hh * 128, hh * 128 + 128) for hh in headsA])
        rowsB = np.concatenate([np.arange(hh * 128, hh * 128 + 128) for hh in headsB])
        return np.ascontiguousarray(WoT[rowsA]), np.ascontiguousarray(WoT[rowsB])

    shared = {"cos2": cos2, "sin2": sin2, "ident": ident, "rmat": rmat, "tri": tri}
    per_layer = {}
    for l in lids:
        if l == 0:
            Wq = np.asarray(inputs["L0_attn_Wq"], f32)[perm]
            Wk = np.asarray(inputs["L0_attn_Wk"], f32)[perm]
            Wv = np.asarray(inputs["L0_attn_Wv"], f32)
            woA, woB = woparts(np.asarray(inputs["L0_attn_Wo"], f32))
            g = np.ascontiguousarray(np.asarray(inputs["L0_ffn_w_gate"], f32).T)
            u = np.ascontiguousarray(np.asarray(inputs["L0_ffn_w_up"], f32).T)
            d = np.ascontiguousarray(np.asarray(inputs["L0_ffn_w_down"], f32).T)
            per_layer[l] = dict(Wq=Wq, Wk=Wk, Wv=Wv, woA=woA, woB=woB, g=g, u=u, d=d)
        else:
            Wq = np.asarray(inputs["L1_attn_Wq"], f32)
            Wk = np.asarray(inputs["L1_attn_Wk"], f32)
            Wv = np.asarray(inputs["L1_attn_Wv"], f32)
            woA, woB = woparts(np.asarray(inputs["L1_attn_Wo"], f32))
            g = np.ascontiguousarray(np.asarray(inputs["L1_ffn_gate_w"], f32).T)
            u = np.ascontiguousarray(np.asarray(inputs["L1_ffn_up_w"], f32).T)
            d = np.ascontiguousarray(np.asarray(inputs["L1_ffn_down_w"], f32).T)
            nht = LCFG[1]["HF"] // 128
            gs = np.ascontiguousarray(np.asarray(inputs["L1_ffn_gate_s"], f32).reshape(nht, 128).T)
            us = np.ascontiguousarray(np.asarray(inputs["L1_ffn_up_s"], f32).reshape(nht, 128).T)
            dsl = np.ascontiguousarray(np.asarray(inputs["L1_ffn_down_s"], f32).reshape(16, 128).T)
            per_layer[l] = dict(Wq=Wq, Wk=Wk, Wv=Wv, woA=woA, woB=woB, g=g, u=u, d=d,
                                gs=gs, us=us, ds=dsl)

    in_maps = []
    for c in range(8):
        b, r = c // 4, c % 4
        m = dict(shared)
        m["xin"] = np.ascontiguousarray(X[b, r * TC:(r + 1) * TC, :].T)
        for l in lids:
            pl = per_layer[l]
            hs = 4 * r
            m[f"wq{l}"] = np.ascontiguousarray(pl["Wq"][hs * 128:(hs + 4) * 128].T)
            m[f"wk{l}"] = np.ascontiguousarray(pl["Wk"][hs * 128:(hs + 4) * 128].T)
            m[f"wv{l}"] = np.ascontiguousarray(pl["Wv"][hs * 128:(hs + 4) * 128].T)
            m[f"woA{l}"] = pl["woA"]
            m[f"woB{l}"] = pl["woB"]
            m[f"g{l}"] = pl["g"]
            m[f"u{l}"] = pl["u"]
            m[f"d{l}"] = pl["d"]
            if LCFG[l]["scales"]:
                m[f"gs{l}"] = pl["gs"]
                m[f"us{l}"] = pl["us"]
                m[f"ds{l}"] = pl["ds"]
        in_maps.append(m)
    return in_maps


def run(inputs, lids=(0, 1), phases=("ln", "ffn", "proj", "attn", "wo", "cc")):
    in_maps = _prep(inputs, lids)
    nc = build_nc(lids, phases)
    res = run_bass_kernel_spmd(nc, in_maps, core_ids=list(range(8)))
    out = np.empty((B, T, D), dtype=np.float32)
    for c in range(8):
        b, r = c // 4, c % 4
        out[b, r * TC:(r + 1) * TC, :] = res.results[c]["xout"].T
    return out


def kernel(**inputs):
    return run(inputs, lids=(0, 1))
